# version 02
# baseline (speedup 1.0000x reference)
import math
from contextlib import ExitStack

import numpy as np
import ml_dtypes

import concourse.bass as bass
import concourse.mybir as mybir
from concourse.bass_utils import run_bass_kernel_spmd

F32 = mybir.dt.float32
BF16 = mybir.dt.bfloat16
AF = mybir.ActivationFunctionType
ALU = mybir.AluOpType

S, D, NH, HD, C = 4096, 1024, 8, 128, 128
NCH = S // C
EPS = 1e-5


def host_constants():
    gam = 1.0 - np.exp(np.linspace(math.log(1.0 / 32.0), math.log(1.0 / 512.0), NH))
    ii = np.arange(C)
    diff = ii[None, :] - ii[:, None]
    dloc = np.where(diff[:, :, None] >= 0,
                    gam[None, None, :] ** np.maximum(diff, 0)[:, :, None], 0.0)
    DLOC = np.ascontiguousarray(dloc.transpose(0, 2, 1)).astype(np.float32)
    GQ = np.ascontiguousarray(np.broadcast_to((gam[:, None] ** (ii[None, :] + 1))[None],
                                              (128, NH, C))).astype(np.float32)
    GK = np.ascontiguousarray(np.broadcast_to((gam[:, None] ** (C - 1 - ii[None, :]))[None],
                                              (128, NH, C))).astype(np.float32)
    GC = [float(np.float32(gam[h] ** C)) for h in range(NH)]

    half = HD // 2
    scale = (np.arange(0, HD, 2, dtype=np.float64) + 0.4 * HD) / (1.4 * HD)
    pos = np.arange(S, dtype=np.float64)
    sc = scale[None, :] ** (pos[:, None] / 512.0)
    inv_freq = 1.0 / 10000.0 ** (np.arange(half, dtype=np.float64) / half)
    sinus = pos[:, None] * inv_freq[None, :]
    sin, cos = np.sin(sinus), np.cos(sinus)

    def rep2(a):
        return np.repeat(a, 2, axis=-1)

    cosQ, sinQ = rep2(cos * sc), rep2(sin * sc)
    cosK, sinK = rep2(cos / sc), rep2(sin / sc)
    TAB = np.empty((NCH, 128, 2, 2, C), np.float32)
    for ci in range(NCH):
        sl = slice(ci * C, (ci + 1) * C)
        TAB[ci, :, 0, 0, :] = cosQ[sl].T
        TAB[ci, :, 1, 0, :] = sinQ[sl].T
        TAB[ci, :, 0, 1, :] = cosK[sl].T
        TAB[ci, :, 1, 1, :] = sinK[sl].T

    P = np.zeros((128, 128), np.float64)
    for m in range(64):
        P[2 * m, 2 * m + 1] = -1.0
        P[2 * m + 1, 2 * m] = 1.0
    PT = np.ascontiguousarray(P.T).astype(ml_dtypes.bfloat16)
    IDN = np.eye(128, dtype=ml_dtypes.bfloat16)
    IDNF = np.eye(128, dtype=np.float32)
    ONES = np.ones((1, 128), ml_dtypes.bfloat16)
    return dict(TAB=TAB, DLOC=DLOC, GQ=GQ, GK=GK, PT=PT, IDN=IDN, IDNF=IDNF, ONES=ONES), GC


class Sched:

    def __init__(self, nc, sems):
        self.nc = nc
        self.sems = sems
        self.cnt = {k: 0 for k in sems}
        self.ops = []
        self.sb_w = {}
        self.sb_r = {}
        self.ps_w_bytes = []
        self.ps_w_bank = {}
        self.ps_r_bank = {}
        self.waited = {}

    @staticmethod
    def banks(lo, hi):
        return range(lo // 512, (hi + 511) // 512)

    def op(self, eng, fn, reads=(), writes=(), pr=(), pw=(), dma=False, tag=""):
        sem = ("d_" if dma else "m_") + eng
        deps = {}

        def need(tok, force=False):
            if tok is None:
                return
            s, v = tok
            if s.startswith("d_"):
                v = self.cnt[s]
            if v > deps.get(s, 0):
                deps[s] = v

        for b in reads:
            need(self.sb_w.get(b))
        for b in writes:
            need(self.sb_w.get(b))
            for s2, v2 in self.sb_r.get(b, {}).items():
                need((s2, v2))
        for (lo, hi) in pr:
            for bk in self.banks(lo, hi):
                for s2, v2 in self.ps_w_bank.get(bk, {}).items():
                    need((s2, v2), force=True)
                for s2, v2 in self.ps_r_bank.get(bk, {}).items():
                    if s2 != sem:
                        need((s2, v2), force=True)
        for (lo, hi) in pw:
            assert eng == "tensor", f"only PE writes PSUM (op {tag})"
            for (l2, h2, s2, v2) in self.ps_w_bytes:
                if l2 < hi and lo < h2:
                    need((s2, v2), force=True)
            for bk in self.banks(lo, hi):
                for s2, v2 in self.ps_r_bank.get(bk, {}).items():
                    need((s2, v2), force=True)

        waits = []
        for s, v in sorted(deps.items()):
            if self.waited.get((eng, s), 0) < v:
                waits.append((s, v))
                self.waited[(eng, s)] = v

        self.cnt[sem] += 16 if dma else 1
        tok = (sem, self.cnt[sem])
        for b in reads:
            self.sb_r.setdefault(b, {})[sem] = tok[1]
        for b in writes:
            self.sb_w[b] = tok
            self.sb_r[b] = {}
        for (lo, hi) in pr:
            for bk in self.banks(lo, hi):
                self.ps_r_bank.setdefault(bk, {})[sem] = tok[1]
        for (lo, hi) in pw:
            self.ps_w_bytes = [e for e in self.ps_w_bytes if not (lo <= e[0] and e[1] <= hi)]
            self.ps_w_bytes.append((lo, hi, sem, tok[1]))
            for bk in self.banks(lo, hi):
                self.ps_w_bank.setdefault(bk, {})[sem] = tok[1]
        self.ops.append((eng, waits, fn, sem, 16 if dma else 1, tag))
        return tok

    def run(self, block):
        for eng in ("sync", "scalar", "tensor", "vector", "gpsimd"):
            ops = [o for o in self.ops if o[0] == eng]

            def make(ops=ops, eng=eng):
                def go(e):
                    for (_, waits, fn, semname, inc, tag) in ops:
                        for (s, v) in waits:
                            e.wait_ge(self.sems[s], v)
                        inst = fn()
                        inst.then_inc(self.sems[semname], inc)
                    if eng == "scalar":
                        e.wait_ge(self.sems["d_scalar"], self.cnt["d_scalar"])
                return go

            getattr(block, eng)(make())


def build_nc(gc_list, dbg=None):
    dbg = dbg or set()
    nc = bass.Bass("TRN2", target_bir_lowering=False)

    def din(name, shape, dtype=F32):
        return nc.dram_tensor(name, shape, dtype, kind="ExternalInput").ap()

    x_h = din("x", [S, D])
    wq_h = din("wq", [NH, HD, HD]); wk_h = din("wk", [NH, HD, HD]); wv_h = din("wv", [NH, HD, HD])
    wg_h = din("wg", [D, D]); wo_h = din("wo", [D, D])
    gnw_h = din("gnw", [D]); gnb_h = din("gnb", [D])
    u1_h = din("u1", [D, 256]); v1_h = din("v1", [256, 4096]); b1_h = din("b1", [4096])
    u2_h = din("u2", [4096, 256]); v2_h = din("v2", [256, D]); b2_h = din("b2", [D])
    tab_h = din("TAB", [NCH, 128, 2, 2, C])
    dloc_h = din("DLOC", [128, NH, C]); gq_h = din("GQ", [128, NH, C]); gk_h = din("GK", [128, NH, C])
    pt_h = din("PT", [128, 128], BF16); idn_h = din("IDN", [128, 128], BF16)
    idnf_h = din("IDNF", [128, 128]); ones_h = din("ONES", [1, 128], BF16)
    y_h = nc.dram_tensor("y", [S, D], F32, kind="ExternalOutput").ap()

    dbg_names = []

    with ExitStack() as st:
        E = st.enter_context
        wq = E(nc.sbuf_tensor("wq_s", [128, NH, HD], BF16))
        wk = E(nc.sbuf_tensor("wk_s", [128, NH, HD], BF16))
        wv = E(nc.sbuf_tensor("wv_s", [128, NH, HD], BF16))
        wqr = E(nc.sbuf_tensor("wqr_s", [128, NH, HD], BF16))
        wkr = E(nc.sbuf_tensor("wkr_s", [128, NH, HD], BF16))
        wg = E(nc.sbuf_tensor("wg_s", [128, 8, D], BF16))
        wo = E(nc.sbuf_tensor("wo_s", [128, 8, D], BF16))
        u1 = E(nc.sbuf_tensor("u1_s", [128, 8, 256], BF16))
        v1 = E(nc.sbuf_tensor("v1_s", [128, 2, 4096], BF16))
        u2 = E(nc.sbuf_tensor("u2_s", [128, 32, 256], BF16))
        v2 = E(nc.sbuf_tensor("v2_s", [128, 2, D], BF16))
        b2r = E(nc.sbuf_tensor("b2r_s", [1, D], BF16))
        onesb = E(nc.sbuf_tensor("ones_s", [1, 128], BF16))
        gnwbc = E(nc.sbuf_tensor("gnwbc", [128, D], F32))
        gnbbc = E(nc.sbuf_tensor("gnbbc", [128, D], F32))
        b1raw = E(nc.sbuf_tensor("b1raw", [32, 128], F32))
        b1t = E(nc.sbuf_tensor("b1t", [128, 32], F32))
        dloc = E(nc.sbuf_tensor("dloc", [128, NH, C], F32))
        gq = E(nc.sbuf_tensor("gq", [128, NH, C], F32))
        gk = E(nc.sbuf_tensor("gk", [128, NH, C], F32))
        idn = E(nc.sbuf_tensor("idn", [128, 128], BF16))
        idnf = E(nc.sbuf_tensor("idnf", [128, 128], F32))
        pt = E(nc.sbuf_tensor("pt", [128, 128], BF16))
        wtt = E(nc.sbuf_tensor("wtt", [128, 128], BF16))

        x0 = [E(nc.sbuf_tensor(f"x0_{p}", [128, D], F32)) for p in range(2)]
        tabs = [E(nc.sbuf_tensor(f"tab_{p}", [128, 2, 2, C], F32)) for p in range(2)]
        xntm = E(nc.sbuf_tensor("xntm", [128, D], BF16))
        xnfm = E(nc.sbuf_tensor("xnfm", [128, NH, C], BF16))
        qkbf = E(nc.sbuf_tensor("qkbf", [128, NH, 2, C], BF16))
        qtb = E(nc.sbuf_tensor("qtb", [128, NH, C], BF16))
        ktb = E(nc.sbuf_tensor("ktb", [128, NH, C], BF16))
        kttm = E(nc.sbuf_tensor("kttm", [128, NH, C], BF16))
        vbf = E(nc.sbuf_tensor("vbf", [128, NH, C], BF16))
        stb = E(nc.sbuf_tensor("stb", [128, NH, C], BF16))
        Rst = E(nc.sbuf_tensor("Rst", [128, NH, HD], F32))
        Rbf = E(nc.sbuf_tensor("Rbf", [128, NH, HD], BF16))
        scr = E(nc.sbuf_tensor("scr", [128, 2, 3072], F32))
        hbf = E(nc.sbuf_tensor("hbf", [128, NH, C], BF16))
        hfm = E(nc.sbuf_tensor("hfm", [128, NH, C], BF16))
        x1 = [E(nc.sbuf_tensor(f"x1_{p}", [128, D], F32)) for p in range(2)]
        xn2tm = E(nc.sbuf_tensor("xn2tm", [128, D], BF16))
        xn2fm = E(nc.sbuf_tensor("xn2fm", [128, NH, C], BF16))
        pbf = E(nc.sbuf_tensor("pbf", [128, 2, 128], BF16))
        Hbf = E(nc.sbuf_tensor("Hbf", [128, 32, 128], BF16))
        qftmp = E(nc.sbuf_tensor("qftmp", [128, 256], F32))
        qfbf = E(nc.sbuf_tensor("qfbf", [128, 2, 128], BF16))
        outb = [E(nc.sbuf_tensor(f"out_{p}", [128, D], F32)) for p in range(2)]
        st6 = [E(nc.sbuf_tensor(f"st6_{p}", [128, 2, 6], F32)) for p in range(2)]
        st2 = [E(nc.sbuf_tensor(f"st2_{p}", [128, 2], F32)) for p in range(2)]
        st6g = E(nc.sbuf_tensor("st6g", [128, NH, 6], F32))
        gtmp = E(nc.sbuf_tensor("gtmp", [128, 5, NH], F32))
        rstd = E(nc.sbuf_tensor("rstd", [128, 8], F32))
        negmr = E(nc.sbuf_tensor("negmr", [128, 4], F32))
        rstdg = E(nc.sbuf_tensor("rstdg", [128, NH], F32))
        cst = E(nc.sbuf_tensor("cst", [128, 4], F32))
        lnsum = E(nc.sbuf_tensor("lnsum", [128, 4], F32))
        lnsq = E(nc.sbuf_tensor("lnsq", [128, 4], F32))
        lntmp = E(nc.sbuf_tensor("lntmp", [128, 4], F32))
        negmrg = E(nc.sbuf_tensor("negmrg", [128, NH], F32))

        PS = E(nc.psum_tensor("PS", [128, 4096], F32))
        PSB = PS[:].bitcast(BF16)

        sems = {}
        for name in ("m_tensor", "m_vector", "m_scalar", "m_gpsimd", "d_sync", "d_scalar", "d_gpsimd"):
            sems[name] = E(nc.semaphore(name))
        sch = Sched(nc, sems)

        def tap(stage, I, ap_fn, bufname):
            if (stage, I) in dbg:
                ap = ap_fn()
                nm = f"dbg_{stage}_{I}"
                t = nc.dram_tensor(nm, list(ap.shape), ap.dtype, kind="ExternalOutput").ap()
                dbg_names.append(nm)
                sch.op("sync", lambda ap=ap, t=t: nc.sync.dma_start(out=t, in_=ap),
                       reads=[bufname], dma=True, tag=f"tap_{stage}_{I}")

        for fn, wr, tg in (
            (lambda: nc.gpsimd.dma_start(out=wq[:], in_=wq_h.rearrange("h d e -> d h e")), ["wq"], "ld_wq"),
            (lambda: nc.gpsimd.dma_start(out=wk[:], in_=wk_h.rearrange("h d e -> d h e")), ["wk"], "ld_wk"),
            (lambda: nc.gpsimd.dma_start(out=wv[:], in_=wv_h.rearrange("h d e -> d h e")), ["wv"], "ld_wv"),
            (lambda: nc.gpsimd.dma_start(out=wg[:], in_=wg_h.rearrange("(b p) o -> p b o", p=128)), ["wg"], "ld_wg"),
            (lambda: nc.gpsimd.dma_start(out=wo[:], in_=wo_h.rearrange("(b p) o -> p b o", p=128)), ["wo"], "ld_wo"),
            (lambda: nc.gpsimd.dma_start(out=u1[:], in_=u1_h.rearrange("(b p) r -> p b r", p=128)), ["u1"], "ld_u1"),
            (lambda: nc.gpsimd.dma_start(out=v1[:], in_=v1_h.rearrange("(b p) f -> p b f", p=128)), ["v1"], "ld_v1"),
            (lambda: nc.gpsimd.dma_start(out=u2[:], in_=u2_h.rearrange("(b p) r -> p b r", p=128)), ["u2"], "ld_u2"),
            (lambda: nc.gpsimd.dma_start(out=v2[:], in_=v2_h.rearrange("(b p) o -> p b o", p=128)), ["v2"], "ld_v2"),
            (lambda: nc.gpsimd.dma_start(out=b2r[:], in_=b2_h.unsqueeze(0)), ["b2r"], "ld_b2"),
            (lambda: nc.gpsimd.dma_start(out=onesb[:], in_=ones_h), ["ones"], "ld_ones"),
        ):
            sch.op("gpsimd", fn, writes=wr, dma=True, tag=tg)

        for fn, wr, tg in (
            (lambda: nc.sync.dma_start(out=gnwbc[:], in_=gnw_h.unsqueeze(0).broadcast_to([128, D])), ["gnwbc"], "ld_gnw"),
            (lambda: nc.sync.dma_start(out=gnbbc[:], in_=gnb_h.unsqueeze(0).broadcast_to([128, D])), ["gnbbc"], "ld_gnb"),
            (lambda: nc.sync.dma_start(out=b1raw[:], in_=b1_h.rearrange("(a b) -> a b", b=128)), ["b1raw"], "ld_b1"),
            (lambda: nc.sync.dma_start(out=dloc[:], in_=dloc_h), ["dloc"], "ld_dloc"),
            (lambda: nc.sync.dma_start(out=gq[:], in_=gq_h), ["gq"], "ld_gq"),
            (lambda: nc.sync.dma_start(out=gk[:], in_=gk_h), ["gk"], "ld_gk"),
            (lambda: nc.sync.dma_start(out=idn[:], in_=idn_h), ["idn"], "ld_idn"),
            (lambda: nc.sync.dma_start(out=idnf[:], in_=idnf_h), ["idnf"], "ld_idnf"),
            (lambda: nc.sync.dma_start(out=pt[:], in_=pt_h), ["pt"], "ld_pt"),
        ):
            sch.op("sync", fn, writes=wr, dma=True, tag=tg)

        for (src, dst, sname, dname) in ((wq, wqr, "wq", "wqr"), (wk, wkr, "wk", "wkr")):
            for h in range(NH):
                sch.op("tensor", lambda src=src, h=h: nc.tensor.transpose(PSB[:, 0:128], src[:, h, :], idn[:]),
                       reads=[sname, "idn"], pw=[(0, 64)], tag=f"T_{dname}{h}")
                sch.op("vector", lambda: nc.vector.tensor_copy(out=wtt[:], in_=PSB[:, 0:128]),
                       writes=["wtt"], pr=[(0, 64)], tag=f"ev_{dname}{h}")
                sch.op("tensor", lambda: nc.tensor.matmul(PS[:, 512:640], wtt[:], pt[:], start=True, stop=True),
                       reads=["wtt", "pt"], pw=[(512, 640)], tag=f"mm_{dname}{h}")
                sch.op("vector", lambda dst=dst, h=h: nc.vector.tensor_copy(out=dst[:, h, :], in_=PS[:, 512:640]),
                       writes=[dname], pr=[(512, 640)], tag=f"ev2_{dname}{h}")

        tap("wqr", 0, lambda: wqr[:], "wqr")
        tap("wkr", 0, lambda: wkr[:], "wkr")
        sch.op("tensor", lambda: nc.tensor.transpose(PS[:, 1024:1056], b1raw[:], idnf[0:32, 0:32]),
               reads=["b1raw", "idnf"], pw=[(1024, 1056)], tag="T_b1")
        sch.op("vector", lambda: nc.vector.tensor_copy(out=b1t[:], in_=PS[:, 1024:1056]),
               writes=["b1t"], pr=[(1024, 1056)], tag="ev_b1")
        sch.op("vector", lambda: nc.vector.memset(Rst[:], 0.0), writes=["Rst"], tag="memset_R")
        sch.op("vector", lambda: (nc.vector.memset(cst[:, 0:1], EPS),
                                  nc.vector.memset(cst[:, 1:2], -0.5),
                                  nc.vector.memset(cst[:, 2:3], 0.25),
                                  nc.vector.memset(cst[:, 3:4], 1.0 / D))[-1],
               writes=["cst"], tag="memset_cst")

        def emit_loads(I):
            p = I % 2
            sch.op("sync", lambda I=I, p=p: nc.sync.dma_start(out=x0[p][:], in_=x_h[I * C:(I + 1) * C, :]),
                   writes=[f"x0{p}"], dma=True, tag=f"ld_x0_{I}")
            sch.op("sync", lambda I=I, p=p: nc.sync.dma_start(out=tabs[p][:], in_=tab_h[I]),
                   writes=[f"tab{p}"], dma=True, tag=f"ld_tab_{I}")

        emit_loads(0)

        def sacol(h):
            return 1024 + (h % 2) * 512 + (h // 2) * 128

        def ycol(h):
            return 2048 + (h % 2) * 512 + (h // 2) * 128

        def emit_ln_stats(I, p, srcfn, src_reads, src_pr, idx, tagp):
            sch.op("scalar", lambda: nc.scalar.activation(out=scr[:, p, 1024:2048], in_=srcfn(),
                                                          func=AF.Square, accum_out=lnsq[:, idx:idx + 1]),
                   reads=src_reads, writes=[f"sB{p}", f"lnsq{idx}"], pr=src_pr, tag=f"sq{tagp}_{I}")
            sch.op("vector", lambda: nc.vector.tensor_scalar(out=scr[:, p, 0:1024], in0=srcfn(),
                                                             scalar1=1.0, scalar2=None, op0=ALU.mult,
                                                             op1=ALU.add, accum_out=lnsum[:, idx:idx + 1]),
                   reads=src_reads, writes=[f"sA{p}", f"lnsum{idx}"], pr=src_pr, tag=f"sm{tagp}_{I}")
            sch.op("vector", lambda: nc.vector.tensor_tensor(out=lntmp[:, idx:idx + 1], in0=lnsum[:, idx:idx + 1],
                                                             in1=lnsum[:, idx:idx + 1], op=ALU.mult),
                   reads=[f"lnsum{idx}"], writes=[f"lntmp{idx}"], tag=f"varA{tagp}_{I}")
            sch.op("vector", lambda: nc.vector.scalar_tensor_tensor(out=lntmp[:, idx:idx + 1],
                                                                    in0=lntmp[:, idx:idx + 1],
                                                                    scalar=-1.0 / D, in1=lnsq[:, idx:idx + 1],
                                                                    op0=ALU.mult, op1=ALU.add),
                   reads=[f"lntmp{idx}", f"lnsq{idx}"], writes=[f"lntmp{idx}"], tag=f"varB{tagp}_{I}")
            sch.op("scalar", lambda: nc.scalar.activation(out=rstd[:, 4 + idx:5 + idx], in_=lntmp[:, idx:idx + 1],
                                                          func=AF.Ln, bias=cst[:, 0:1], scale=cst[:, 3:4]),
                   reads=[f"lntmp{idx}", "cst"], writes=[f"rstdL{idx}"], tag=f"rstdA{tagp}_{I}")
            sch.op("scalar", lambda: nc.scalar.activation(out=rstd[:, idx:idx + 1], in_=rstd[:, 4 + idx:5 + idx],
                                                          func=AF.Exp, scale=cst[:, 1:2]),
                   reads=[f"rstdL{idx}", "cst"], writes=[f"rstd{idx}"], tag=f"rstdB{tagp}_{I}")
            sch.op("vector", lambda: nc.vector.tensor_scalar(out=negmr[:, idx:idx + 1], in0=lnsum[:, idx:idx + 1],
                                                             scalar1=rstd[:, idx:idx + 1], scalar2=-1.0 / D,
                                                             op0=ALU.mult, op1=ALU.mult),
                   reads=[f"lnsum{idx}", f"rstd{idx}"], writes=[f"negmr{idx}"], tag=f"negmr{tagp}_{I}")

        for I in range(NCH):
            p = I % 2
            if I + 1 < NCH:
                emit_loads(I + 1)

            emit_ln_stats(I, p, lambda p=p: x0[p][:], [f"x0{p}"], [], 0, "1")
            sch.op("vector", lambda p=p: nc.vector.tensor_scalar(out=xntm[:], in0=x0[p][:],
                                                                 scalar1=rstd[:, 0:1], scalar2=negmr[:, 0:1],
                                                                 op0=ALU.mult, op1=ALU.add),
                   reads=[f"x0{p}", "rstd0", "negmr0"], writes=["xntm"], tag=f"ln1ap_{I}")
            tap("xntm", I, lambda: xntm[:], "xntm")

            sch.op("tensor", lambda: [nc.tensor.transpose(PSB[:, 6144 + h * 128:6272 + h * 128],
                                                          xntm[:, h * 128:(h + 1) * 128], idn[:])
                                      for h in range(NH)][-1],
                   reads=["xntm", "idn"], pw=[(3072 + h * 64, 3136 + h * 64) for h in range(NH)], tag=f"xnT_{I}")
            sch.op("vector", lambda: nc.vector.tensor_copy(out=xnfm[:], in_=PSB[:, 6144:7168]),
                   writes=["xnfm"], pr=[(3072, 3584)], tag=f"evxn_{I}")
            tap("xnfm", I, lambda: xnfm[:], "xnfm")

            def qkmm(base, wa, wb, h):
                c0 = base + h * 256
                nc.tensor.matmul(PS[:, c0:c0 + 128], wa[:, h, :], xnfm[:, h, :],
                                 start=(h % 2 == 0), stop=False)
                return nc.tensor.matmul(PS[:, c0 + 128:c0 + 256], wb[:, h, :], xnfm[:, h, :],
                                        start=False, stop=(h % 2 == 1))
            sch.op("tensor", lambda: [qkmm(0, wq, wk, h) for h in range(NH)][-1],
                   reads=["xnfm", "wq", "wk"], pw=[(0, 2048)], tag=f"qk_{I}")
            sch.op("tensor", lambda: [qkmm(2048, wqr, wkr, h) for h in range(NH)][-1],
                   reads=["xnfm", "wqr", "wkr"], pw=[(2048, 4096)], tag=f"qkr_{I}")

            for g in range(2):
                qv = PS[:, g * 1024:(g + 1) * 1024].rearrange("p (h s t) -> p h s t", h=4, s=2)
                rv = PS[:, 2048 + g * 1024:3072 + g * 1024].rearrange("p (h s t) -> p h s t", h=4, s=2)
                cosb = tabs[p][:, 0, :, :].unsqueeze(1).broadcast_to([128, 4, 2, C])
                sinb = tabs[p][:, 1, :, :].unsqueeze(1).broadcast_to([128, 4, 2, C])
                t1v = scr[:, p, 0:1024].rearrange("p (h s t) -> p h s t", h=4, s=2)
                t2v = scr[:, p, 1024:2048].rearrange("p (h s t) -> p h s t", h=4, s=2)
                sch.op("vector", lambda qv=qv, cosb=cosb, t1v=t1v:
                       nc.vector.tensor_tensor(out=t1v, in0=qv, in1=cosb, op=ALU.mult),
                       reads=[f"tab{p}"], writes=[f"sA{p}"], pr=[(g * 1024, (g + 1) * 1024)], tag=f"t1_{I}_{g}")
                sch.op("vector", lambda rv=rv, sinb=sinb, t2v=t2v:
                       nc.vector.tensor_tensor(out=t2v, in0=rv, in1=sinb, op=ALU.mult),
                       reads=[f"tab{p}"], writes=[f"sB{p}"], pr=[(2048 + g * 1024, 3072 + g * 1024)], tag=f"t2_{I}_{g}")
                sch.op("gpsimd", lambda g=g, p=p: nc.gpsimd.tensor_tensor(
                           out=qkbf[:, 4 * g:4 * g + 4, :, :],
                           in0=scr[:, p, 0:1024], in1=scr[:, p, 1024:2048], op=ALU.add),
                       reads=[f"sA{p}", f"sB{p}"], writes=["qkbf"], tag=f"qkadd_{I}_{g}")
            tap("qkbf", I, lambda: qkbf[:], "qkbf")

            sch.op("gpsimd", lambda: nc.gpsimd.tensor_tensor(out=qtb[:], in0=qkbf[:, :, 0, :], in1=gq[:], op=ALU.mult),
                   reads=["qkbf", "gq"], writes=["qtb"], tag=f"qt_{I}")
            sch.op("gpsimd", lambda: nc.gpsimd.tensor_tensor(out=ktb[:], in0=qkbf[:, :, 1, :], in1=gk[:], op=ALU.mult),
                   reads=["qkbf", "gk"], writes=["ktb"], tag=f"kt_{I}")

            sch.op("tensor", lambda: [nc.tensor.matmul(PS[:, h * 128:(h + 1) * 128], xnfm[:, h, :], wv[:, h, :],
                                                       start=True, stop=True) for h in range(NH)][-1],
                   reads=["xnfm", "wv"], pw=[(0, 1024)], tag=f"v_{I}")
            sch.op("vector", lambda: nc.vector.tensor_copy(out=vbf[:], in_=PS[:, 0:1024]),
                   writes=["vbf"], pr=[(0, 1024)], tag=f"evv_{I}")
            tap("vbf", I, lambda: vbf[:], "vbf")

            sch.op("tensor", lambda: [nc.tensor.transpose(PSB[:, 6144 + h * 128:6272 + h * 128],
                                                          ktb[:, h, :], idn[:]) for h in range(NH)][-1],
                   reads=["ktb", "idn"], pw=[(3072 + h * 64, 3136 + h * 64) for h in range(NH)], tag=f"ktT_{I}")
            sch.op("vector", lambda: nc.vector.tensor_copy(out=kttm[:], in_=PSB[:, 6144:7168]),
                   writes=["kttm"], pr=[(3072, 3584)], tag=f"evkt_{I}")

            sch.op("tensor", lambda: [nc.tensor.matmul(PS[:, sacol(h):sacol(h) + 128],
                                                       qkbf[:, h, 1, :], qkbf[:, h, 0, :],
                                                       start=True, stop=True) for h in range(NH)][-1],
                   reads=["qkbf"], pw=[(sacol(h), sacol(h) + 128) for h in range(NH)], tag=f"s_{I}")
            for h in range(NH):
                sch.op("vector", lambda h=h: nc.vector.tensor_tensor(
                           out=stb[:, h, :], in0=PS[:, sacol(h):sacol(h) + 128], in1=dloc[:, h, :], op=ALU.mult),
                       reads=["dloc"], writes=[f"stb{h}"], pr=[(sacol(h), sacol(h) + 128)], tag=f"st_{I}_{h}")
            tap("stb", I, lambda: stb[:], "stb7")

            for q in range(4):
                h0, h1 = 2 * q, 2 * q + 1
                for h in (h0, h1):
                    sch.op("tensor", lambda h=h, I=I: nc.tensor.matmul(
                               PS[:, ycol(h):ycol(h) + 128], stb[:, h, :], vbf[:, h, :],
                               start=True, stop=(I == 0)),
                           reads=[f"stb{h}", "vbf"], pw=[(ycol(h), ycol(h) + 128)], tag=f"yi_{I}_{h}")
                for h in (h0, h1):
                    sch.op("tensor", lambda h=h: nc.tensor.matmul(
                               PS[:, sacol(h):sacol(h) + 128], kttm[:, h, :], vbf[:, h, :],
                               start=True, stop=True),
                           reads=["kttm", "vbf"], pw=[(sacol(h), sacol(h) + 128)], tag=f"a_{I}_{h}")
                sch.op("tensor", lambda q=q: (nc.tensor.matmul(PS[:, 3072:3584], xnfm[:, 2 * q, :],
                                                               wg[:, 2 * q, 0:512], start=(q == 0), stop=False),
                                              nc.tensor.matmul(PS[:, 3584:4096], xnfm[:, 2 * q, :],
                                                               wg[:, 2 * q, 512:1024], start=(q == 0), stop=False))[-1],
                       reads=["xnfm", "wg"], pw=[(3072, 4096)], tag=f"g_{I}_{2 * q}")
                if I > 0:
                    for h in (h0, h1):
                        sch.op("tensor", lambda h=h: nc.tensor.matmul(
                                   PS[:, ycol(h):ycol(h) + 128], qtb[:, h, :], Rbf[:, h, :],
                                   start=False, stop=True),
                               reads=["qtb", "Rbf"], pw=[(ycol(h), ycol(h) + 128)], tag=f"yc_{I}_{h}")
                sch.op("tensor", lambda q=q: (nc.tensor.matmul(PS[:, 3072:3584], xnfm[:, 2 * q + 1, :],
                                                               wg[:, 2 * q + 1, 0:512], start=False, stop=(q == 3)),
                                              nc.tensor.matmul(PS[:, 3584:4096], xnfm[:, 2 * q + 1, :],
                                                               wg[:, 2 * q + 1, 512:1024], start=False, stop=(q == 3)))[-1],
                       reads=["xnfm", "wg"], pw=[(3072, 4096)], tag=f"g_{I}_{2 * q + 1}")

            for h in range(NH):
                sch.op("vector", lambda h=h: nc.vector.scalar_tensor_tensor(
                           out=Rst[:, h, :], in0=Rst[:, h, :], scalar=gc_list[h],
                           in1=PS[:, sacol(h):sacol(h) + 128], op0=ALU.mult, op1=ALU.add),
                       reads=["Rst"], writes=["Rst"], pr=[(sacol(h), sacol(h) + 128)], tag=f"rup_{I}_{h}")
            sch.op("gpsimd", lambda: nc.gpsimd.tensor_copy(out=Rbf[:], in_=Rst[:]),
                   reads=["Rst"], writes=["Rbf"], tag=f"rbf_{I}")

            for h in range(NH):
                sch.op("vector", lambda h=h: nc.vector.bn_stats(out=st6g[:, h, :], in_=PS[:, ycol(h):ycol(h) + 128]),
                       writes=[f"st6g{h}", "st6g7"] if h == NH - 1 else [f"st6g{h}"],
                       pr=[(ycol(h), ycol(h) + 128)], tag=f"bng_{I}_{h}")
            allst = [f"st6g{h}" for h in range(NH)]
            sch.op("vector", lambda: nc.vector.tensor_tensor(out=gtmp[:, 0, :], in0=st6g[:, :, 1],
                                                             in1=st6g[:, :, 4], op=ALU.add),
                   reads=allst, writes=["gtA"], tag=f"gnc1_{I}")
            sch.op("vector", lambda: nc.vector.tensor_tensor(out=gtmp[:, 1, :], in0=st6g[:, :, 1],
                                                             in1=st6g[:, :, 4], op=ALU.subtract),
                   reads=allst, writes=["gtD"], tag=f"gnc2_{I}")
            sch.op("vector", lambda: nc.vector.tensor_tensor(out=gtmp[:, 2, :], in0=gtmp[:, 1, :],
                                                             in1=gtmp[:, 1, :], op=ALU.mult),
                   reads=["gtD"], writes=["gtDD"], tag=f"gnc3_{I}")
            sch.op("vector", lambda: nc.vector.tensor_tensor(out=gtmp[:, 3, :], in0=st6g[:, :, 2],
                                                             in1=st6g[:, :, 5], op=ALU.add),
                   reads=allst, writes=["gtS"], tag=f"gnc4_{I}")
            sch.op("vector", lambda: nc.vector.scalar_tensor_tensor(out=gtmp[:, 4, :], in0=gtmp[:, 3, :],
                                                                    scalar=4.0 / HD, in1=gtmp[:, 2, :],
                                                                    op0=ALU.mult, op1=ALU.add),
                   reads=["gtS", "gtDD"], writes=["gtV"], tag=f"gnc5_{I}")
            sch.op("scalar", lambda: nc.scalar.activation(out=rstdg[:], in_=gtmp[:, 4, :], func=AF.Ln,
                                                          bias=cst[:, 0:1], scale=cst[:, 2:3]),
                   reads=["gtV", "cst"], writes=["rstdg"], tag=f"rstdgA_{I}")
            sch.op("scalar", lambda: nc.scalar.activation(out=rstdg[:], in_=rstdg[:], func=AF.Exp, scale=cst[:, 1:2]),
                   reads=["cst", "rstdg"], writes=["rstdg"], tag=f"rstdgB_{I}")
            sch.op("vector", lambda: nc.vector.scalar_tensor_tensor(out=negmrg[:], in0=gtmp[:, 0, :], scalar=-0.5,
                                                                    in1=rstdg[:], op0=ALU.mult, op1=ALU.mult),
                   reads=["gtA", "rstdg"], writes=["negmrg"], tag=f"negmrg_{I}")

            for h in range(NH):
                sch.op("vector", lambda h=h, p=p: nc.vector.tensor_scalar(
                           out=scr[:, p, h * 128:(h + 1) * 128], in0=PS[:, ycol(h):ycol(h) + 128],
                           scalar1=rstdg[:, h:h + 1], scalar2=negmrg[:, h:h + 1], op0=ALU.mult, op1=ALU.add),
                       reads=["rstdg", "negmrg"], writes=[f"sA{p}"],
                       pr=[(ycol(h), ycol(h) + 128)], tag=f"gnap_{I}_{h}")
            tap("st6g", I, lambda: st6g[:], "st6g7")
            tap("rstdg", I, lambda: rstdg[:], "rstdg")
            tap("negmrg", I, lambda: negmrg[:], "negmrg")
            sch.op("scalar", lambda p=p: nc.scalar.activation(out=scr[:, p, 1024:2048], in_=PS[:, 3072:4096],
                                                              func=AF.Silu),
                   writes=[f"sB{p}"], pr=[(3072, 4096)], tag=f"silu_{I}")
            tap("gnsilu", I, lambda p=p: scr[:, p, 1024:2048], f"sB{p}")
            sch.op("gpsimd", lambda p=p: nc.gpsimd.tensor_tensor(out=scr[:, p, 2048:3072], in0=scr[:, p, 0:1024],
                                                                 in1=gnwbc[:], op=ALU.mult),
                   reads=[f"sA{p}", "gnwbc"], writes=[f"sC{p}"], tag=f"gnw_{I}")
            sch.op("gpsimd", lambda p=p: nc.gpsimd.tensor_tensor(out=scr[:, p, 0:1024], in0=scr[:, p, 2048:3072],
                                                                 in1=gnbbc[:], op=ALU.add),
                   reads=[f"sC{p}", "gnbbc"], writes=[f"sA{p}"], tag=f"gnb_{I}")
            sch.op("vector", lambda p=p: nc.vector.tensor_tensor(out=hbf[:], in0=scr[:, p, 0:1024],
                                                                 in1=scr[:, p, 1024:2048], op=ALU.mult),
                   reads=[f"sA{p}", f"sB{p}"], writes=["hbf"], tag=f"gate_{I}")
            tap("hbf", I, lambda: hbf[:], "hbf")

            sch.op("tensor", lambda: [nc.tensor.transpose(PSB[:, 2048 + h * 128:2176 + h * 128],
                                                          hbf[:, h, :], idn[:]) for h in range(NH)][-1],
                   reads=["hbf", "idn"], pw=[(1024 + h * 64, 1088 + h * 64) for h in range(NH)], tag=f"hT_{I}")
            sch.op("vector", lambda: nc.vector.tensor_copy(out=hfm[:], in_=PSB[:, 2048:3072]),
                   writes=["hfm"], pr=[(1024, 1536)], tag=f"evh_{I}")
            for hb in range(NH):
                sch.op("tensor", lambda hb=hb: (nc.tensor.matmul(PS[:, 0:512], hfm[:, hb, :], wo[:, hb, 0:512],
                                                                 start=(hb == 0), stop=(hb == NH - 1)),
                                                nc.tensor.matmul(PS[:, 512:1024], hfm[:, hb, :], wo[:, hb, 512:1024],
                                                                 start=(hb == 0), stop=(hb == NH - 1)))[-1],
                       reads=["hfm", "wo"], pw=[(0, 1024)], tag=f"wo_{I}_{hb}")

            emit_ln_stats(I, p, lambda: PS[:, 0:1024], [], [(0, 1024)], 1, "2")
            sch.op("vector", lambda p=p: nc.vector.tensor_scalar(out=scr[:, p, 2048:3072], in0=PS[:, 0:1024],
                                                                 scalar1=rstd[:, 1:2], scalar2=negmr[:, 1:2],
                                                                 op0=ALU.mult, op1=ALU.add),
                   reads=["rstd1", "negmr1"], writes=[f"sC{p}"], pr=[(0, 1024)], tag=f"ln2ap_{I}")
            sch.op("gpsimd", lambda p=p: nc.gpsimd.tensor_tensor(out=x1[p][:], in0=scr[:, p, 2048:3072],
                                                                 in1=x0[p][:], op=ALU.add),
                   reads=[f"sC{p}", f"x0{p}"], writes=[f"x1{p}"], tag=f"resid1_{I}")
            tap("x1", I, lambda p=p: x1[p][:], f"x1{p}")

            emit_ln_stats(I, p, lambda p=p: x1[p][:], [f"x1{p}"], [], 2, "3")
            sch.op("vector", lambda p=p: nc.vector.tensor_scalar(out=xn2tm[:], in0=x1[p][:],
                                                                 scalar1=rstd[:, 2:3], scalar2=negmr[:, 2:3],
                                                                 op0=ALU.mult, op1=ALU.add),
                   reads=[f"x1{p}", "rstd2", "negmr2"], writes=["xn2tm"], tag=f"ln3ap_{I}")
            sch.op("tensor", lambda: [nc.tensor.transpose(PSB[:, 3072 + h * 128:3200 + h * 128],
                                                          xn2tm[:, h * 128:(h + 1) * 128], idn[:])
                                      for h in range(NH)][-1],
                   reads=["xn2tm", "idn"], pw=[(1536 + h * 64, 1600 + h * 64) for h in range(NH)], tag=f"xn2T_{I}")
            sch.op("vector", lambda: nc.vector.tensor_copy(out=xn2fm[:], in_=PSB[:, 3072:4096]),
                   writes=["xn2fm"], pr=[(1536, 2048)], tag=f"evxn2_{I}")
            tap("xn2fm", I, lambda: xn2fm[:], "xn2fm")

            for k in range(8):
                base = 1024 + (k % 2) * 512
                sch.op("tensor", lambda k=k, base=base: (
                           nc.tensor.matmul(PS[:, base:base + 128], u1[:, k, 0:128], xn2fm[:, k, :],
                                            start=(k < 2), stop=False),
                           nc.tensor.matmul(PS[:, base + 128:base + 256], u1[:, k, 128:256], xn2fm[:, k, :],
                                            start=False, stop=(k >= 6)))[-1],
                       reads=["xn2fm", "u1"], pw=[(base, base + 256)], tag=f"p_{I}_{k}")
            sch.op("vector", lambda: nc.vector.tensor_copy(out=qftmp[:], in_=PS[:, 1024:1280]),
                   writes=["qftmp"], pr=[(1024, 1280)], tag=f"pev1_{I}")
            sch.op("vector", lambda: nc.vector.tensor_tensor(out=pbf[:], in0=qftmp[:],
                                                             in1=PS[:, 1536:1792], op=ALU.add),
                   reads=["qftmp"], writes=["pbf"], pr=[(1536, 1792)], tag=f"pev2_{I}")
            tap("pbf", I, lambda: pbf[:], "pbf")

            qf_cols = (0, 512, 1024, 1536)
            for half in range(2):
                fbs = list(range(half * 16, half * 16 + 16))
                for rc in range(2):
                    sch.op("tensor", lambda rc=rc, fbs=fbs: [
                               nc.tensor.matmul(PS[:, 2048 + (fb % 16) * 128:2176 + (fb % 16) * 128],
                                                v1[:, rc, fb * 128:(fb + 1) * 128], pbf[:, rc, :],
                                                start=(rc == 0 and fb % 4 == 0),
                                                stop=(rc == 1 and fb % 4 == 3)) for fb in fbs][-1],
                           reads=["pbf", "v1"], pw=[(2048, 4096)], tag=f"h_{I}_{half}_{rc}")
                sch.op("scalar", lambda fbs=fbs: [
                           nc.scalar.activation(out=Hbf[:, fb, :],
                                                in_=PS[:, 2048 + (fb % 16) * 128:2176 + (fb % 16) * 128],
                                                func=AF.Gelu_apprx_tanh, bias=b1t[:, fb:fb + 1]) for fb in fbs][-1],
                       reads=["b1t"], writes=[f"Hbf{half}"], pr=[(2048, 4096)], tag=f"gelu_{I}_{half}")
                for fb in fbs:
                    c0 = qf_cols[fb % 4]
                    sch.op("tensor", lambda fb=fb, c0=c0, half=half: (
                               nc.tensor.matmul(PS[:, c0:c0 + 128], u2[:, fb, 0:128], Hbf[:, fb, :],
                                                start=(fb < 4), stop=False),
                               nc.tensor.matmul(PS[:, c0 + 128:c0 + 256], u2[:, fb, 128:256], Hbf[:, fb, :],
                                                start=False, stop=(fb >= 28)))[-1],
                           reads=[f"Hbf{half}", "u2"], pw=[(c0, c0 + 256)], tag=f"u2_{I}_{fb}")

            sch.op("vector", lambda: nc.vector.tensor_copy(out=qftmp[:], in_=PS[:, 0:256]),
                   writes=["qftmp"], pr=[(0, 256)], tag=f"qfc1_{I}")
            sch.op("vector", lambda: nc.vector.tensor_tensor(out=qftmp[:], in0=qftmp[:], in1=PS[:, 512:768], op=ALU.add),
                   reads=["qftmp"], writes=["qftmp"], pr=[(512, 768)], tag=f"qfc2_{I}")
            sch.op("vector", lambda: nc.vector.tensor_tensor(out=qftmp[:], in0=qftmp[:], in1=PS[:, 1024:1280], op=ALU.add),
                   reads=["qftmp"], writes=["qftmp"], pr=[(1024, 1280)], tag=f"qfc3_{I}")
            sch.op("vector", lambda: nc.vector.tensor_tensor(out=qfbf[:], in0=qftmp[:], in1=PS[:, 1536:1792], op=ALU.add),
                   reads=["qftmp"], writes=["qfbf"], pr=[(1536, 1792)], tag=f"qfc4_{I}")
            tap("qfbf", I, lambda: qfbf[:], "qfbf")

            sch.op("tensor", lambda: (nc.tensor.matmul(PS[:, 3072:3584], qfbf[:, 0, :], v2[:, 0, 0:512],
                                                       start=True, stop=False),
                                      nc.tensor.matmul(PS[:, 3584:4096], qfbf[:, 0, :], v2[:, 0, 512:1024],
                                                       start=True, stop=False))[-1],
                   reads=["qfbf", "v2"], pw=[(3072, 4096)], tag=f"v2a_{I}")
            sch.op("tensor", lambda: (nc.tensor.matmul(PS[:, 3072:3584], qfbf[:, 1, :], v2[:, 1, 0:512],
                                                       start=False, stop=False),
                                      nc.tensor.matmul(PS[:, 3584:4096], qfbf[:, 1, :], v2[:, 1, 512:1024],
                                                       start=False, stop=False))[-1],
                   reads=["qfbf", "v2"], pw=[(3072, 4096)], tag=f"v2b_{I}")
            sch.op("tensor", lambda: (nc.tensor.matmul(PS[:, 3072:3584], onesb[:], b2r[0:1, 0:512],
                                                       start=False, stop=True),
                                      nc.tensor.matmul(PS[:, 3584:4096], onesb[:], b2r[0:1, 512:1024],
                                                       start=False, stop=True))[-1],
                   reads=["ones", "b2r"], pw=[(3072, 4096)], tag=f"b2_{I}")

            emit_ln_stats(I, p, lambda: PS[:, 3072:4096], [], [(3072, 4096)], 3, "4")
            sch.op("vector", lambda p=p: nc.vector.tensor_scalar(out=scr[:, p, 2048:3072], in0=PS[:, 3072:4096],
                                                                 scalar1=rstd[:, 3:4], scalar2=negmr[:, 3:4],
                                                                 op0=ALU.mult, op1=ALU.add),
                   reads=["rstd3", "negmr3"], writes=[f"sC{p}"], pr=[(3072, 4096)], tag=f"ln4ap_{I}")
            sch.op("gpsimd", lambda p=p: nc.gpsimd.tensor_tensor(out=outb[p][:], in0=scr[:, p, 2048:3072],
                                                                 in1=x1[p][:], op=ALU.add),
                   reads=[f"sC{p}", f"x1{p}"], writes=[f"out{p}"], tag=f"resid2_{I}")
            sch.op("scalar", lambda I=I, p=p: nc.scalar.dma_start(out=y_h[I * C:(I + 1) * C, :], in_=outb[p][:]),
                   reads=[f"out{p}"], dma=True, tag=f"store_{I}")

        block = E(nc.Block())
        sch.run(block)

    return nc, dbg_names


def kernel(L0_attn_X, L0_attn_W_Q, L0_attn_W_K, L0_attn_W_V, L0_attn_W_G, L0_attn_W_O,
           L0_attn_gn_weight, L0_attn_gn_bias, L0_ffn_U1, L0_ffn_V1, L0_ffn_b1,
           L0_ffn_U2, L0_ffn_V2, L0_ffn_b2):
    consts, gc_list = host_constants()
    nc, _ = build_nc(gc_list)
    common = dict(
        wq=np.ascontiguousarray(L0_attn_W_Q, np.float32),
        wk=np.ascontiguousarray(L0_attn_W_K, np.float32),
        wv=np.ascontiguousarray(L0_attn_W_V, np.float32),
        wg=np.ascontiguousarray(L0_attn_W_G, np.float32),
        wo=np.ascontiguousarray(L0_attn_W_O, np.float32),
        gnw=np.ascontiguousarray(L0_attn_gn_weight, np.float32),
        gnb=np.ascontiguousarray(L0_attn_gn_bias, np.float32),
        u1=np.ascontiguousarray(L0_ffn_U1, np.float32),
        v1=np.ascontiguousarray(L0_ffn_V1, np.float32),
        b1=np.ascontiguousarray(L0_ffn_b1, np.float32),
        u2=np.ascontiguousarray(L0_ffn_U2, np.float32),
        v2=np.ascontiguousarray(L0_ffn_V2, np.float32),
        b2=np.ascontiguousarray(L0_ffn_b2, np.float32),
        **consts,
    )
    X = np.ascontiguousarray(L0_attn_X, np.float32)
    in_maps = [dict(common, x=np.ascontiguousarray(X[c])) for c in range(8)]
    res = run_bass_kernel_spmd(nc, in_maps, core_ids=list(range(8)))
    return np.stack([np.asarray(r["y"], np.float32) for r in res.results], axis=0)


# version 03
# speedup vs baseline: 1.0013x; 1.0013x over previous
import math
from contextlib import ExitStack

import numpy as np
import ml_dtypes

import concourse.bass as bass
import concourse.mybir as mybir
from concourse.bass_utils import run_bass_kernel_spmd

F32 = mybir.dt.float32
BF16 = mybir.dt.bfloat16
AF = mybir.ActivationFunctionType
ALU = mybir.AluOpType

S, D, NH, HD, C = 4096, 1024, 8, 128, 128
NCH = S // C
EPS = 1e-5


def host_constants():
    gam = 1.0 - np.exp(np.linspace(math.log(1.0 / 32.0), math.log(1.0 / 512.0), NH))
    ii = np.arange(C)
    diff = ii[None, :] - ii[:, None]
    dloc = np.where(diff[:, :, None] >= 0,
                    gam[None, None, :] ** np.maximum(diff, 0)[:, :, None], 0.0)
    DLOC = np.ascontiguousarray(dloc.transpose(0, 2, 1)).astype(np.float32)
    GQ = np.ascontiguousarray(np.broadcast_to((gam[:, None] ** (ii[None, :] + 1))[None],
                                              (128, NH, C))).astype(np.float32)
    GK = np.ascontiguousarray(np.broadcast_to((gam[:, None] ** (C - 1 - ii[None, :]))[None],
                                              (128, NH, C))).astype(np.float32)
    GC = [float(np.float32(gam[h] ** C)) for h in range(NH)]

    half = HD // 2
    scale = (np.arange(0, HD, 2, dtype=np.float64) + 0.4 * HD) / (1.4 * HD)
    pos = np.arange(S, dtype=np.float64)
    sc = scale[None, :] ** (pos[:, None] / 512.0)
    inv_freq = 1.0 / 10000.0 ** (np.arange(half, dtype=np.float64) / half)
    sinus = pos[:, None] * inv_freq[None, :]
    sin, cos = np.sin(sinus), np.cos(sinus)

    def rep2(a):
        return np.repeat(a, 2, axis=-1)

    cosQ, sinQ = rep2(cos * sc), rep2(sin * sc)
    cosK, sinK = rep2(cos / sc), rep2(sin / sc)
    TAB = np.empty((NCH, 128, 2, 2, C), np.float32)
    for ci in range(NCH):
        sl = slice(ci * C, (ci + 1) * C)
        TAB[ci, :, 0, 0, :] = cosQ[sl].T
        TAB[ci, :, 1, 0, :] = sinQ[sl].T
        TAB[ci, :, 0, 1, :] = cosK[sl].T
        TAB[ci, :, 1, 1, :] = sinK[sl].T

    P = np.zeros((128, 128), np.float64)
    for m in range(64):
        P[2 * m, 2 * m + 1] = -1.0
        P[2 * m + 1, 2 * m] = 1.0
    PT = np.ascontiguousarray(P.T).astype(ml_dtypes.bfloat16)
    IDN = np.eye(128, dtype=ml_dtypes.bfloat16)
    IDNF = np.eye(128, dtype=np.float32)
    ONES = np.ones((1, 128), ml_dtypes.bfloat16)
    return dict(TAB=TAB, DLOC=DLOC, GQ=GQ, GK=GK, PT=PT, IDN=IDN, IDNF=IDNF, ONES=ONES), GC


class Sched:

    def __init__(self, nc, sems):
        self.nc = nc
        self.sems = sems
        self.cnt = {k: 0 for k in sems}
        self.ops = []
        self.sb_w = {}
        self.sb_r = {}
        self.ps_w_bytes = []
        self.ps_w_bank = {}
        self.ps_r_bank = {}
        self.waited = {}

    @staticmethod
    def banks(lo, hi):
        return range(lo // 512, (hi + 511) // 512)

    def op(self, eng, fn, reads=(), writes=(), pr=(), pw=(), dma=False, tag=""):
        sem = ("d_" if dma else "m_") + eng
        deps = {}

        def need(tok, force=False):
            if tok is None:
                return
            s, v = tok
            if v > deps.get(s, 0):
                deps[s] = v

        for b in reads:
            need(self.sb_w.get(b))
        for b in writes:
            need(self.sb_w.get(b))
            for s2, v2 in self.sb_r.get(b, {}).items():
                need((s2, v2))
        for (lo, hi) in pr:
            for bk in self.banks(lo, hi):
                for s2, v2 in self.ps_w_bank.get(bk, {}).items():
                    need((s2, v2), force=True)
                for s2, v2 in self.ps_r_bank.get(bk, {}).items():
                    if s2 != sem:
                        need((s2, v2), force=True)
        for (lo, hi) in pw:
            assert eng == "tensor", f"only PE writes PSUM (op {tag})"
            for (l2, h2, s2, v2) in self.ps_w_bytes:
                if l2 < hi and lo < h2:
                    need((s2, v2), force=True)
            for bk in self.banks(lo, hi):
                for s2, v2 in self.ps_r_bank.get(bk, {}).items():
                    need((s2, v2), force=True)

        if dma and self.cnt[sem] > 0:
            deps[sem] = max(deps.get(sem, 0), self.cnt[sem])

        waits = []
        for s, v in sorted(deps.items()):
            if self.waited.get((eng, s), 0) < v:
                waits.append((s, v))
                self.waited[(eng, s)] = v

        self.cnt[sem] += 16 if dma else 1
        tok = (sem, self.cnt[sem])
        for b in reads:
            self.sb_r.setdefault(b, {})[sem] = tok[1]
        for b in writes:
            self.sb_w[b] = tok
            self.sb_r[b] = {}
        for (lo, hi) in pr:
            for bk in self.banks(lo, hi):
                self.ps_r_bank.setdefault(bk, {})[sem] = tok[1]
        for (lo, hi) in pw:
            self.ps_w_bytes = [e for e in self.ps_w_bytes if not (lo <= e[0] and e[1] <= hi)]
            self.ps_w_bytes.append((lo, hi, sem, tok[1]))
            for bk in self.banks(lo, hi):
                self.ps_w_bank.setdefault(bk, {})[sem] = tok[1]
        self.ops.append((eng, waits, fn, sem, 16 if dma else 1, tag))
        return tok

    def run(self, block):
        for eng in ("sync", "scalar", "tensor", "vector", "gpsimd"):
            ops = [o for o in self.ops if o[0] == eng]

            def make(ops=ops, eng=eng):
                def go(e):
                    for (_, waits, fn, semname, inc, tag) in ops:
                        for (s, v) in waits:
                            e.wait_ge(self.sems[s], v)
                        inst = fn()
                        inst.then_inc(self.sems[semname], inc)
                    if eng == "scalar":
                        e.wait_ge(self.sems["d_scalar"], self.cnt["d_scalar"])
                return go

            getattr(block, eng)(make())


def build_nc(gc_list, dbg=None):
    dbg = dbg or set()
    nc = bass.Bass("TRN2", target_bir_lowering=False)

    def din(name, shape, dtype=F32):
        return nc.dram_tensor(name, shape, dtype, kind="ExternalInput").ap()

    x_h = din("x", [S, D])
    wq_h = din("wq", [NH, HD, HD]); wk_h = din("wk", [NH, HD, HD]); wv_h = din("wv", [NH, HD, HD])
    wg_h = din("wg", [D, D]); wo_h = din("wo", [D, D])
    gnw_h = din("gnw", [D]); gnb_h = din("gnb", [D])
    u1_h = din("u1", [D, 256]); v1_h = din("v1", [256, 4096]); b1_h = din("b1", [4096])
    u2_h = din("u2", [4096, 256]); v2_h = din("v2", [256, D]); b2_h = din("b2", [D])
    tab_h = din("TAB", [NCH, 128, 2, 2, C])
    dloc_h = din("DLOC", [128, NH, C]); gq_h = din("GQ", [128, NH, C]); gk_h = din("GK", [128, NH, C])
    pt_h = din("PT", [128, 128], BF16); idn_h = din("IDN", [128, 128], BF16)
    idnf_h = din("IDNF", [128, 128]); ones_h = din("ONES", [1, 128], BF16)
    y_h = nc.dram_tensor("y", [S, D], F32, kind="ExternalOutput").ap()

    dbg_names = []

    with ExitStack() as st:
        E = st.enter_context
        wq = E(nc.sbuf_tensor("wq_s", [128, NH, HD], BF16))
        wk = E(nc.sbuf_tensor("wk_s", [128, NH, HD], BF16))
        wv = E(nc.sbuf_tensor("wv_s", [128, NH, HD], BF16))
        wqr = E(nc.sbuf_tensor("wqr_s", [128, NH, HD], BF16))
        wkr = E(nc.sbuf_tensor("wkr_s", [128, NH, HD], BF16))
        wg = E(nc.sbuf_tensor("wg_s", [128, 8, D], BF16))
        wo = E(nc.sbuf_tensor("wo_s", [128, 8, D], BF16))
        u1 = E(nc.sbuf_tensor("u1_s", [128, 8, 256], BF16))
        v1 = E(nc.sbuf_tensor("v1_s", [128, 2, 4096], BF16))
        u2 = E(nc.sbuf_tensor("u2_s", [128, 32, 256], BF16))
        v2 = E(nc.sbuf_tensor("v2_s", [128, 2, D], BF16))
        b2r = E(nc.sbuf_tensor("b2r_s", [1, D], BF16))
        onesb = E(nc.sbuf_tensor("ones_s", [1, 128], BF16))
        gnwbc = E(nc.sbuf_tensor("gnwbc", [128, D], F32))
        gnbbc = E(nc.sbuf_tensor("gnbbc", [128, D], F32))
        b1raw = E(nc.sbuf_tensor("b1raw", [32, 128], F32))
        b1t = E(nc.sbuf_tensor("b1t", [128, 32], F32))
        dloc = E(nc.sbuf_tensor("dloc", [128, NH, C], F32))
        gq = E(nc.sbuf_tensor("gq", [128, NH, C], F32))
        gk = E(nc.sbuf_tensor("gk", [128, NH, C], F32))
        idn = E(nc.sbuf_tensor("idn", [128, 128], BF16))
        idnf = E(nc.sbuf_tensor("idnf", [128, 128], F32))
        pt = E(nc.sbuf_tensor("pt", [128, 128], BF16))
        wtt = E(nc.sbuf_tensor("wtt", [128, 128], BF16))

        x0 = [E(nc.sbuf_tensor(f"x0_{p}", [128, D], F32)) for p in range(2)]
        tabs = [E(nc.sbuf_tensor(f"tab_{p}", [128, 2, 2, C], F32)) for p in range(2)]
        xntm = E(nc.sbuf_tensor("xntm", [128, D], BF16))
        xnfm = E(nc.sbuf_tensor("xnfm", [128, NH, C], BF16))
        qkbf = E(nc.sbuf_tensor("qkbf", [128, NH, 2, C], BF16))
        qtb = E(nc.sbuf_tensor("qtb", [128, NH, C], BF16))
        ktb = E(nc.sbuf_tensor("ktb", [128, NH, C], BF16))
        kttm = E(nc.sbuf_tensor("kttm", [128, NH, C], BF16))
        vbf = E(nc.sbuf_tensor("vbf", [128, NH, C], BF16))
        stb = E(nc.sbuf_tensor("stb", [128, NH, C], BF16))
        Rst = E(nc.sbuf_tensor("Rst", [128, NH, HD], F32))
        Rbf = E(nc.sbuf_tensor("Rbf", [128, NH, HD], BF16))
        scr = E(nc.sbuf_tensor("scr", [128, 2, 3072], F32))
        hbf = E(nc.sbuf_tensor("hbf", [128, NH, C], BF16))
        hfm = E(nc.sbuf_tensor("hfm", [128, NH, C], BF16))
        x1 = [E(nc.sbuf_tensor(f"x1_{p}", [128, D], F32)) for p in range(2)]
        xn2tm = E(nc.sbuf_tensor("xn2tm", [128, D], BF16))
        xn2fm = E(nc.sbuf_tensor("xn2fm", [128, NH, C], BF16))
        pbf = E(nc.sbuf_tensor("pbf", [128, 2, 128], BF16))
        Hbf = E(nc.sbuf_tensor("Hbf", [128, 32, 128], BF16))
        qftmp = E(nc.sbuf_tensor("qftmp", [128, 256], F32))
        qfbf = E(nc.sbuf_tensor("qfbf", [128, 2, 128], BF16))
        outb = [E(nc.sbuf_tensor(f"out_{p}", [128, D], F32)) for p in range(2)]
        st6 = [E(nc.sbuf_tensor(f"st6_{p}", [128, 2, 6], F32)) for p in range(2)]
        st2 = [E(nc.sbuf_tensor(f"st2_{p}", [128, 2], F32)) for p in range(2)]
        st6g = E(nc.sbuf_tensor("st6g", [128, NH, 6], F32))
        gtmp = E(nc.sbuf_tensor("gtmp", [128, 5, NH], F32))
        rstd = E(nc.sbuf_tensor("rstd", [128, 8], F32))
        negmr = E(nc.sbuf_tensor("negmr", [128, 4], F32))
        rstdg = E(nc.sbuf_tensor("rstdg", [128, NH], F32))
        cst = E(nc.sbuf_tensor("cst", [128, 4], F32))
        lnsum = E(nc.sbuf_tensor("lnsum", [128, 4], F32))
        lnsq = E(nc.sbuf_tensor("lnsq", [128, 4], F32))
        lntmp = E(nc.sbuf_tensor("lntmp", [128, 4], F32))
        negmrg = E(nc.sbuf_tensor("negmrg", [128, NH], F32))

        PS = E(nc.psum_tensor("PS", [128, 4096], F32))
        PSB = PS[:].bitcast(BF16)

        sems = {}
        for name in ("m_tensor", "m_vector", "m_scalar", "m_gpsimd", "d_sync", "d_scalar", "d_gpsimd"):
            sems[name] = E(nc.semaphore(name))
        sch = Sched(nc, sems)

        def tap(stage, I, ap_fn, bufname):
            if (stage, I) in dbg:
                ap = ap_fn()
                nm = f"dbg_{stage}_{I}"
                t = nc.dram_tensor(nm, list(ap.shape), ap.dtype, kind="ExternalOutput").ap()
                dbg_names.append(nm)
                sch.op("sync", lambda ap=ap, t=t: nc.sync.dma_start(out=t, in_=ap),
                       reads=[bufname], dma=True, tag=f"tap_{stage}_{I}")

        for fn, wr, tg in (
            (lambda: nc.gpsimd.dma_start(out=wq[:], in_=wq_h.rearrange("h d e -> d h e")), ["wq"], "ld_wq"),
            (lambda: nc.gpsimd.dma_start(out=wk[:], in_=wk_h.rearrange("h d e -> d h e")), ["wk"], "ld_wk"),
            (lambda: nc.gpsimd.dma_start(out=wv[:], in_=wv_h.rearrange("h d e -> d h e")), ["wv"], "ld_wv"),
            (lambda: nc.gpsimd.dma_start(out=wg[:], in_=wg_h.rearrange("(b p) o -> p b o", p=128)), ["wg"], "ld_wg"),
            (lambda: nc.gpsimd.dma_start(out=wo[:], in_=wo_h.rearrange("(b p) o -> p b o", p=128)), ["wo"], "ld_wo"),
            (lambda: nc.gpsimd.dma_start(out=u1[:], in_=u1_h.rearrange("(b p) r -> p b r", p=128)), ["u1"], "ld_u1"),
            (lambda: nc.gpsimd.dma_start(out=v1[:], in_=v1_h.rearrange("(b p) f -> p b f", p=128)), ["v1"], "ld_v1"),
            (lambda: nc.gpsimd.dma_start(out=u2[:], in_=u2_h.rearrange("(b p) r -> p b r", p=128)), ["u2"], "ld_u2"),
            (lambda: nc.gpsimd.dma_start(out=v2[:], in_=v2_h.rearrange("(b p) o -> p b o", p=128)), ["v2"], "ld_v2"),
            (lambda: nc.gpsimd.dma_start(out=b2r[:], in_=b2_h.unsqueeze(0)), ["b2r"], "ld_b2"),
            (lambda: nc.gpsimd.dma_start(out=onesb[:], in_=ones_h), ["ones"], "ld_ones"),
        ):
            sch.op("gpsimd", fn, writes=wr, dma=True, tag=tg)

        for fn, wr, tg in (
            (lambda: nc.sync.dma_start(out=gnwbc[:], in_=gnw_h.unsqueeze(0).broadcast_to([128, D])), ["gnwbc"], "ld_gnw"),
            (lambda: nc.sync.dma_start(out=gnbbc[:], in_=gnb_h.unsqueeze(0).broadcast_to([128, D])), ["gnbbc"], "ld_gnb"),
            (lambda: nc.sync.dma_start(out=b1raw[:], in_=b1_h.rearrange("(a b) -> a b", b=128)), ["b1raw"], "ld_b1"),
            (lambda: nc.sync.dma_start(out=dloc[:], in_=dloc_h), ["dloc"], "ld_dloc"),
            (lambda: nc.sync.dma_start(out=gq[:], in_=gq_h), ["gq"], "ld_gq"),
            (lambda: nc.sync.dma_start(out=gk[:], in_=gk_h), ["gk"], "ld_gk"),
            (lambda: nc.sync.dma_start(out=idn[:], in_=idn_h), ["idn"], "ld_idn"),
            (lambda: nc.sync.dma_start(out=idnf[:], in_=idnf_h), ["idnf"], "ld_idnf"),
            (lambda: nc.sync.dma_start(out=pt[:], in_=pt_h), ["pt"], "ld_pt"),
        ):
            sch.op("sync", fn, writes=wr, dma=True, tag=tg)

        for (src, dst, sname, dname) in ((wq, wqr, "wq", "wqr"), (wk, wkr, "wk", "wkr")):
            for h in range(NH):
                sch.op("tensor", lambda src=src, h=h: nc.tensor.transpose(PSB[:, 0:128], src[:, h, :], idn[:]),
                       reads=[sname, "idn"], pw=[(0, 64)], tag=f"T_{dname}{h}")
                sch.op("vector", lambda: nc.vector.tensor_copy(out=wtt[:], in_=PSB[:, 0:128]),
                       writes=["wtt"], pr=[(0, 64)], tag=f"ev_{dname}{h}")
                sch.op("tensor", lambda: nc.tensor.matmul(PS[:, 512:640], wtt[:], pt[:], start=True, stop=True),
                       reads=["wtt", "pt"], pw=[(512, 640)], tag=f"mm_{dname}{h}")
                sch.op("vector", lambda dst=dst, h=h: nc.vector.tensor_copy(out=dst[:, h, :], in_=PS[:, 512:640]),
                       writes=[dname], pr=[(512, 640)], tag=f"ev2_{dname}{h}")

        tap("wqr", 0, lambda: wqr[:], "wqr")
        tap("wkr", 0, lambda: wkr[:], "wkr")
        sch.op("tensor", lambda: nc.tensor.transpose(PS[:, 1024:1056], b1raw[:], idnf[0:32, 0:32]),
               reads=["b1raw", "idnf"], pw=[(1024, 1056)], tag="T_b1")
        sch.op("vector", lambda: nc.vector.tensor_copy(out=b1t[:], in_=PS[:, 1024:1056]),
               writes=["b1t"], pr=[(1024, 1056)], tag="ev_b1")
        sch.op("vector", lambda: nc.vector.memset(Rst[:], 0.0), writes=["Rst"], tag="memset_R")
        sch.op("vector", lambda: (nc.vector.memset(cst[:, 0:1], EPS),
                                  nc.vector.memset(cst[:, 1:2], -0.5),
                                  nc.vector.memset(cst[:, 2:3], 0.25),
                                  nc.vector.memset(cst[:, 3:4], 1.0 / D))[-1],
               writes=["cst"], tag="memset_cst")

        def emit_loads(I):
            p = I % 2
            sch.op("sync", lambda I=I, p=p: nc.sync.dma_start(out=x0[p][:], in_=x_h[I * C:(I + 1) * C, :]),
                   writes=[f"x0{p}"], dma=True, tag=f"ld_x0_{I}")
            sch.op("sync", lambda I=I, p=p: nc.sync.dma_start(out=tabs[p][:], in_=tab_h[I]),
                   writes=[f"tab{p}"], dma=True, tag=f"ld_tab_{I}")

        emit_loads(0)

        def sacol(h):
            return 1024 + (h % 2) * 512 + (h // 2) * 128

        def ycol(h):
            return 2048 + (h % 2) * 512 + (h // 2) * 128

        def emit_ln_stats(I, p, srcfn, src_reads, src_pr, idx, tagp):
            sch.op("scalar", lambda: nc.scalar.activation(out=scr[:, p, 1024:2048], in_=srcfn(),
                                                          func=AF.Square, accum_out=lnsq[:, idx:idx + 1]),
                   reads=src_reads, writes=[f"sB{p}", f"lnsq{idx}"], pr=src_pr, tag=f"sq{tagp}_{I}")
            sch.op("vector", lambda: nc.vector.tensor_scalar(out=scr[:, p, 0:1024], in0=srcfn(),
                                                             scalar1=1.0, scalar2=None, op0=ALU.mult,
                                                             op1=ALU.add, accum_out=lnsum[:, idx:idx + 1]),
                   reads=src_reads, writes=[f"sA{p}", f"lnsum{idx}"], pr=src_pr, tag=f"sm{tagp}_{I}")
            sch.op("vector", lambda: nc.vector.tensor_tensor(out=lntmp[:, idx:idx + 1], in0=lnsum[:, idx:idx + 1],
                                                             in1=lnsum[:, idx:idx + 1], op=ALU.mult),
                   reads=[f"lnsum{idx}"], writes=[f"lntmp{idx}"], tag=f"varA{tagp}_{I}")
            sch.op("vector", lambda: nc.vector.scalar_tensor_tensor(out=lntmp[:, idx:idx + 1],
                                                                    in0=lntmp[:, idx:idx + 1],
                                                                    scalar=-1.0 / D, in1=lnsq[:, idx:idx + 1],
                                                                    op0=ALU.mult, op1=ALU.add),
                   reads=[f"lntmp{idx}", f"lnsq{idx}"], writes=[f"lntmp{idx}"], tag=f"varB{tagp}_{I}")
            sch.op("scalar", lambda: nc.scalar.activation(out=rstd[:, 4 + idx:5 + idx], in_=lntmp[:, idx:idx + 1],
                                                          func=AF.Ln, bias=cst[:, 0:1], scale=cst[:, 3:4]),
                   reads=[f"lntmp{idx}", "cst"], writes=[f"rstdL{idx}"], tag=f"rstdA{tagp}_{I}")
            sch.op("scalar", lambda: nc.scalar.activation(out=rstd[:, idx:idx + 1], in_=rstd[:, 4 + idx:5 + idx],
                                                          func=AF.Exp, scale=cst[:, 1:2]),
                   reads=[f"rstdL{idx}", "cst"], writes=[f"rstd{idx}"], tag=f"rstdB{tagp}_{I}")
            sch.op("vector", lambda: nc.vector.tensor_scalar(out=negmr[:, idx:idx + 1], in0=lnsum[:, idx:idx + 1],
                                                             scalar1=rstd[:, idx:idx + 1], scalar2=-1.0 / D,
                                                             op0=ALU.mult, op1=ALU.mult),
                   reads=[f"lnsum{idx}", f"rstd{idx}"], writes=[f"negmr{idx}"], tag=f"negmr{tagp}_{I}")

        for I in range(NCH):
            p = I % 2
            if I + 1 < NCH:
                emit_loads(I + 1)

            emit_ln_stats(I, p, lambda p=p: x0[p][:], [f"x0{p}"], [], 0, "1")
            sch.op("vector", lambda p=p: nc.vector.tensor_scalar(out=xntm[:], in0=x0[p][:],
                                                                 scalar1=rstd[:, 0:1], scalar2=negmr[:, 0:1],
                                                                 op0=ALU.mult, op1=ALU.add),
                   reads=[f"x0{p}", "rstd0", "negmr0"], writes=["xntm"], tag=f"ln1ap_{I}")
            tap("xntm", I, lambda: xntm[:], "xntm")

            sch.op("tensor", lambda: [nc.tensor.transpose(PSB[:, 6144 + h * 128:6272 + h * 128],
                                                          xntm[:, h * 128:(h + 1) * 128], idn[:])
                                      for h in range(NH)][-1],
                   reads=["xntm", "idn"], pw=[(3072 + h * 64, 3136 + h * 64) for h in range(NH)], tag=f"xnT_{I}")
            sch.op("vector", lambda: nc.vector.tensor_copy(out=xnfm[:], in_=PSB[:, 6144:7168]),
                   writes=["xnfm"], pr=[(3072, 3584)], tag=f"evxn_{I}")
            tap("xnfm", I, lambda: xnfm[:], "xnfm")

            def qkmm(base, wa, wb, h):
                c0 = base + h * 256
                nc.tensor.matmul(PS[:, c0:c0 + 128], wa[:, h, :], xnfm[:, h, :],
                                 start=(h % 2 == 0), stop=False)
                return nc.tensor.matmul(PS[:, c0 + 128:c0 + 256], wb[:, h, :], xnfm[:, h, :],
                                        start=False, stop=(h % 2 == 1))
            sch.op("tensor", lambda: [qkmm(0, wq, wk, h) for h in range(NH)][-1],
                   reads=["xnfm", "wq", "wk"], pw=[(0, 2048)], tag=f"qk_{I}")
            sch.op("tensor", lambda: [qkmm(2048, wqr, wkr, h) for h in range(NH)][-1],
                   reads=["xnfm", "wqr", "wkr"], pw=[(2048, 4096)], tag=f"qkr_{I}")

            for g in range(2):
                qv = PS[:, g * 1024:(g + 1) * 1024].rearrange("p (h s t) -> p h s t", h=4, s=2)
                rv = PS[:, 2048 + g * 1024:3072 + g * 1024].rearrange("p (h s t) -> p h s t", h=4, s=2)
                cosb = tabs[p][:, 0, :, :].unsqueeze(1).broadcast_to([128, 4, 2, C])
                sinb = tabs[p][:, 1, :, :].unsqueeze(1).broadcast_to([128, 4, 2, C])
                t1v = scr[:, p, 0:1024].rearrange("p (h s t) -> p h s t", h=4, s=2)
                t2v = scr[:, p, 1024:2048].rearrange("p (h s t) -> p h s t", h=4, s=2)
                sch.op("vector", lambda qv=qv, cosb=cosb, t1v=t1v:
                       nc.vector.tensor_tensor(out=t1v, in0=qv, in1=cosb, op=ALU.mult),
                       reads=[f"tab{p}"], writes=[f"sA{p}"], pr=[(g * 1024, (g + 1) * 1024)], tag=f"t1_{I}_{g}")
                sch.op("vector", lambda rv=rv, sinb=sinb, t2v=t2v:
                       nc.vector.tensor_tensor(out=t2v, in0=rv, in1=sinb, op=ALU.mult),
                       reads=[f"tab{p}"], writes=[f"sB{p}"], pr=[(2048 + g * 1024, 3072 + g * 1024)], tag=f"t2_{I}_{g}")
                sch.op("gpsimd", lambda g=g, p=p: nc.gpsimd.tensor_tensor(
                           out=qkbf[:, 4 * g:4 * g + 4, :, :],
                           in0=scr[:, p, 0:1024], in1=scr[:, p, 1024:2048], op=ALU.add),
                       reads=[f"sA{p}", f"sB{p}"], writes=["qkbf"], tag=f"qkadd_{I}_{g}")
            tap("qkbf", I, lambda: qkbf[:], "qkbf")

            sch.op("gpsimd", lambda: nc.gpsimd.tensor_tensor(out=qtb[:], in0=qkbf[:, :, 0, :], in1=gq[:], op=ALU.mult),
                   reads=["qkbf", "gq"], writes=["qtb"], tag=f"qt_{I}")
            sch.op("gpsimd", lambda: nc.gpsimd.tensor_tensor(out=ktb[:], in0=qkbf[:, :, 1, :], in1=gk[:], op=ALU.mult),
                   reads=["qkbf", "gk"], writes=["ktb"], tag=f"kt_{I}")

            sch.op("tensor", lambda: [nc.tensor.matmul(PS[:, h * 128:(h + 1) * 128], xnfm[:, h, :], wv[:, h, :],
                                                       start=True, stop=True) for h in range(NH)][-1],
                   reads=["xnfm", "wv"], pw=[(0, 1024)], tag=f"v_{I}")
            sch.op("vector", lambda: nc.vector.tensor_copy(out=vbf[:], in_=PS[:, 0:1024]),
                   writes=["vbf"], pr=[(0, 1024)], tag=f"evv_{I}")
            tap("vbf", I, lambda: vbf[:], "vbf")

            sch.op("tensor", lambda: [nc.tensor.transpose(PSB[:, 6144 + h * 128:6272 + h * 128],
                                                          ktb[:, h, :], idn[:]) for h in range(NH)][-1],
                   reads=["ktb", "idn"], pw=[(3072 + h * 64, 3136 + h * 64) for h in range(NH)], tag=f"ktT_{I}")
            sch.op("vector", lambda: nc.vector.tensor_copy(out=kttm[:], in_=PSB[:, 6144:7168]),
                   writes=["kttm"], pr=[(3072, 3584)], tag=f"evkt_{I}")

            sch.op("tensor", lambda: [nc.tensor.matmul(PS[:, sacol(h):sacol(h) + 128],
                                                       qkbf[:, h, 1, :], qkbf[:, h, 0, :],
                                                       start=True, stop=True) for h in range(NH)][-1],
                   reads=["qkbf"], pw=[(sacol(h), sacol(h) + 128) for h in range(NH)], tag=f"s_{I}")
            for h in range(NH):
                sch.op("vector", lambda h=h: nc.vector.tensor_tensor(
                           out=stb[:, h, :], in0=PS[:, sacol(h):sacol(h) + 128], in1=dloc[:, h, :], op=ALU.mult),
                       reads=["dloc"], writes=[f"stb{h}"], pr=[(sacol(h), sacol(h) + 128)], tag=f"st_{I}_{h}")
            tap("stb", I, lambda: stb[:], "stb7")

            for q in range(4):
                h0, h1 = 2 * q, 2 * q + 1
                for h in (h0, h1):
                    sch.op("tensor", lambda h=h, I=I: nc.tensor.matmul(
                               PS[:, ycol(h):ycol(h) + 128], stb[:, h, :], vbf[:, h, :],
                               start=True, stop=(I == 0)),
                           reads=[f"stb{h}", "vbf"], pw=[(ycol(h), ycol(h) + 128)], tag=f"yi_{I}_{h}")
                for h in (h0, h1):
                    sch.op("tensor", lambda h=h: nc.tensor.matmul(
                               PS[:, sacol(h):sacol(h) + 128], kttm[:, h, :], vbf[:, h, :],
                               start=True, stop=True),
                           reads=["kttm", "vbf"], pw=[(sacol(h), sacol(h) + 128)], tag=f"a_{I}_{h}")
                sch.op("tensor", lambda q=q: (nc.tensor.matmul(PS[:, 3072:3584], xnfm[:, 2 * q, :],
                                                               wg[:, 2 * q, 0:512], start=(q == 0), stop=False),
                                              nc.tensor.matmul(PS[:, 3584:4096], xnfm[:, 2 * q, :],
                                                               wg[:, 2 * q, 512:1024], start=(q == 0), stop=False))[-1],
                       reads=["xnfm", "wg"], pw=[(3072, 4096)], tag=f"g_{I}_{2 * q}")
                if I > 0:
                    for h in (h0, h1):
                        sch.op("tensor", lambda h=h: nc.tensor.matmul(
                                   PS[:, ycol(h):ycol(h) + 128], qtb[:, h, :], Rbf[:, h, :],
                                   start=False, stop=True),
                               reads=["qtb", "Rbf"], pw=[(ycol(h), ycol(h) + 128)], tag=f"yc_{I}_{h}")
                sch.op("tensor", lambda q=q: (nc.tensor.matmul(PS[:, 3072:3584], xnfm[:, 2 * q + 1, :],
                                                               wg[:, 2 * q + 1, 0:512], start=False, stop=(q == 3)),
                                              nc.tensor.matmul(PS[:, 3584:4096], xnfm[:, 2 * q + 1, :],
                                                               wg[:, 2 * q + 1, 512:1024], start=False, stop=(q == 3)))[-1],
                       reads=["xnfm", "wg"], pw=[(3072, 4096)], tag=f"g_{I}_{2 * q + 1}")

            for h in range(NH):
                sch.op("vector", lambda h=h: nc.vector.scalar_tensor_tensor(
                           out=Rst[:, h, :], in0=Rst[:, h, :], scalar=gc_list[h],
                           in1=PS[:, sacol(h):sacol(h) + 128], op0=ALU.mult, op1=ALU.add),
                       reads=["Rst"], writes=["Rst"], pr=[(sacol(h), sacol(h) + 128)], tag=f"rup_{I}_{h}")
            sch.op("gpsimd", lambda: nc.gpsimd.tensor_copy(out=Rbf[:], in_=Rst[:]),
                   reads=["Rst"], writes=["Rbf"], tag=f"rbf_{I}")

            for h in range(NH):
                sch.op("vector", lambda h=h: nc.vector.bn_stats(out=st6g[:, h, :], in_=PS[:, ycol(h):ycol(h) + 128]),
                       writes=[f"st6g{h}", "st6g7"] if h == NH - 1 else [f"st6g{h}"],
                       pr=[(ycol(h), ycol(h) + 128)], tag=f"bng_{I}_{h}")
            allst = [f"st6g{h}" for h in range(NH)]
            sch.op("vector", lambda: nc.vector.tensor_tensor(out=gtmp[:, 0, :], in0=st6g[:, :, 1],
                                                             in1=st6g[:, :, 4], op=ALU.add),
                   reads=allst, writes=["gtA"], tag=f"gnc1_{I}")
            sch.op("vector", lambda: nc.vector.tensor_tensor(out=gtmp[:, 1, :], in0=st6g[:, :, 1],
                                                             in1=st6g[:, :, 4], op=ALU.subtract),
                   reads=allst, writes=["gtD"], tag=f"gnc2_{I}")
            sch.op("vector", lambda: nc.vector.tensor_tensor(out=gtmp[:, 2, :], in0=gtmp[:, 1, :],
                                                             in1=gtmp[:, 1, :], op=ALU.mult),
                   reads=["gtD"], writes=["gtDD"], tag=f"gnc3_{I}")
            sch.op("vector", lambda: nc.vector.tensor_tensor(out=gtmp[:, 3, :], in0=st6g[:, :, 2],
                                                             in1=st6g[:, :, 5], op=ALU.add),
                   reads=allst, writes=["gtS"], tag=f"gnc4_{I}")
            sch.op("vector", lambda: nc.vector.scalar_tensor_tensor(out=gtmp[:, 4, :], in0=gtmp[:, 3, :],
                                                                    scalar=4.0 / HD, in1=gtmp[:, 2, :],
                                                                    op0=ALU.mult, op1=ALU.add),
                   reads=["gtS", "gtDD"], writes=["gtV"], tag=f"gnc5_{I}")
            sch.op("scalar", lambda: nc.scalar.activation(out=rstdg[:], in_=gtmp[:, 4, :], func=AF.Ln,
                                                          bias=cst[:, 0:1], scale=cst[:, 2:3]),
                   reads=["gtV", "cst"], writes=["rstdg"], tag=f"rstdgA_{I}")
            sch.op("scalar", lambda: nc.scalar.activation(out=rstdg[:], in_=rstdg[:], func=AF.Exp, scale=cst[:, 1:2]),
                   reads=["cst", "rstdg"], writes=["rstdg"], tag=f"rstdgB_{I}")
            sch.op("vector", lambda: nc.vector.scalar_tensor_tensor(out=negmrg[:], in0=gtmp[:, 0, :], scalar=-0.5,
                                                                    in1=rstdg[:], op0=ALU.mult, op1=ALU.mult),
                   reads=["gtA", "rstdg"], writes=["negmrg"], tag=f"negmrg_{I}")

            for h in range(NH):
                sch.op("vector", lambda h=h, p=p: nc.vector.tensor_scalar(
                           out=scr[:, p, h * 128:(h + 1) * 128], in0=PS[:, ycol(h):ycol(h) + 128],
                           scalar1=rstdg[:, h:h + 1], scalar2=negmrg[:, h:h + 1], op0=ALU.mult, op1=ALU.add),
                       reads=["rstdg", "negmrg"], writes=[f"sA{p}"],
                       pr=[(ycol(h), ycol(h) + 128)], tag=f"gnap_{I}_{h}")
            tap("st6g", I, lambda: st6g[:], "st6g7")
            tap("rstdg", I, lambda: rstdg[:], "rstdg")
            tap("negmrg", I, lambda: negmrg[:], "negmrg")
            sch.op("scalar", lambda p=p: nc.scalar.activation(out=scr[:, p, 1024:2048], in_=PS[:, 3072:4096],
                                                              func=AF.Silu),
                   writes=[f"sB{p}"], pr=[(3072, 4096)], tag=f"silu_{I}")
            tap("gnsilu", I, lambda p=p: scr[:, p, 1024:2048], f"sB{p}")
            sch.op("gpsimd", lambda p=p: nc.gpsimd.tensor_tensor(out=scr[:, p, 2048:3072], in0=scr[:, p, 0:1024],
                                                                 in1=gnwbc[:], op=ALU.mult),
                   reads=[f"sA{p}", "gnwbc"], writes=[f"sC{p}"], tag=f"gnw_{I}")
            sch.op("gpsimd", lambda p=p: nc.gpsimd.tensor_tensor(out=scr[:, p, 0:1024], in0=scr[:, p, 2048:3072],
                                                                 in1=gnbbc[:], op=ALU.add),
                   reads=[f"sC{p}", "gnbbc"], writes=[f"sA{p}"], tag=f"gnb_{I}")
            sch.op("vector", lambda p=p: nc.vector.tensor_tensor(out=hbf[:], in0=scr[:, p, 0:1024],
                                                                 in1=scr[:, p, 1024:2048], op=ALU.mult),
                   reads=[f"sA{p}", f"sB{p}"], writes=["hbf"], tag=f"gate_{I}")
            tap("hbf", I, lambda: hbf[:], "hbf")

            sch.op("tensor", lambda: [nc.tensor.transpose(PSB[:, 2048 + h * 128:2176 + h * 128],
                                                          hbf[:, h, :], idn[:]) for h in range(NH)][-1],
                   reads=["hbf", "idn"], pw=[(1024 + h * 64, 1088 + h * 64) for h in range(NH)], tag=f"hT_{I}")
            sch.op("vector", lambda: nc.vector.tensor_copy(out=hfm[:], in_=PSB[:, 2048:3072]),
                   writes=["hfm"], pr=[(1024, 1536)], tag=f"evh_{I}")
            for hb in range(NH):
                sch.op("tensor", lambda hb=hb: (nc.tensor.matmul(PS[:, 0:512], hfm[:, hb, :], wo[:, hb, 0:512],
                                                                 start=(hb == 0), stop=(hb == NH - 1)),
                                                nc.tensor.matmul(PS[:, 512:1024], hfm[:, hb, :], wo[:, hb, 512:1024],
                                                                 start=(hb == 0), stop=(hb == NH - 1)))[-1],
                       reads=["hfm", "wo"], pw=[(0, 1024)], tag=f"wo_{I}_{hb}")

            emit_ln_stats(I, p, lambda: PS[:, 0:1024], [], [(0, 1024)], 1, "2")
            sch.op("vector", lambda p=p: nc.vector.tensor_scalar(out=scr[:, p, 2048:3072], in0=PS[:, 0:1024],
                                                                 scalar1=rstd[:, 1:2], scalar2=negmr[:, 1:2],
                                                                 op0=ALU.mult, op1=ALU.add),
                   reads=["rstd1", "negmr1"], writes=[f"sC{p}"], pr=[(0, 1024)], tag=f"ln2ap_{I}")
            sch.op("gpsimd", lambda p=p: nc.gpsimd.tensor_tensor(out=x1[p][:], in0=scr[:, p, 2048:3072],
                                                                 in1=x0[p][:], op=ALU.add),
                   reads=[f"sC{p}", f"x0{p}"], writes=[f"x1{p}"], tag=f"resid1_{I}")
            tap("x1", I, lambda p=p: x1[p][:], f"x1{p}")

            emit_ln_stats(I, p, lambda p=p: x1[p][:], [f"x1{p}"], [], 2, "3")
            sch.op("vector", lambda p=p: nc.vector.tensor_scalar(out=xn2tm[:], in0=x1[p][:],
                                                                 scalar1=rstd[:, 2:3], scalar2=negmr[:, 2:3],
                                                                 op0=ALU.mult, op1=ALU.add),
                   reads=[f"x1{p}", "rstd2", "negmr2"], writes=["xn2tm"], tag=f"ln3ap_{I}")
            sch.op("tensor", lambda: [nc.tensor.transpose(PSB[:, 3072 + h * 128:3200 + h * 128],
                                                          xn2tm[:, h * 128:(h + 1) * 128], idn[:])
                                      for h in range(NH)][-1],
                   reads=["xn2tm", "idn"], pw=[(1536 + h * 64, 1600 + h * 64) for h in range(NH)], tag=f"xn2T_{I}")
            sch.op("vector", lambda: nc.vector.tensor_copy(out=xn2fm[:], in_=PSB[:, 3072:4096]),
                   writes=["xn2fm"], pr=[(1536, 2048)], tag=f"evxn2_{I}")
            tap("xn2fm", I, lambda: xn2fm[:], "xn2fm")

            for k in range(8):
                base = 1024 + (k % 2) * 512
                sch.op("tensor", lambda k=k, base=base: (
                           nc.tensor.matmul(PS[:, base:base + 128], u1[:, k, 0:128], xn2fm[:, k, :],
                                            start=(k < 2), stop=False),
                           nc.tensor.matmul(PS[:, base + 128:base + 256], u1[:, k, 128:256], xn2fm[:, k, :],
                                            start=False, stop=(k >= 6)))[-1],
                       reads=["xn2fm", "u1"], pw=[(base, base + 256)], tag=f"p_{I}_{k}")
            sch.op("vector", lambda: nc.vector.tensor_copy(out=qftmp[:], in_=PS[:, 1024:1280]),
                   writes=["qftmp"], pr=[(1024, 1280)], tag=f"pev1_{I}")
            sch.op("vector", lambda: nc.vector.tensor_tensor(out=pbf[:], in0=qftmp[:],
                                                             in1=PS[:, 1536:1792], op=ALU.add),
                   reads=["qftmp"], writes=["pbf"], pr=[(1536, 1792)], tag=f"pev2_{I}")
            tap("pbf", I, lambda: pbf[:], "pbf")

            qf_cols = (0, 512, 1024, 1536)
            for half in range(2):
                fbs = list(range(half * 16, half * 16 + 16))
                for rc in range(2):
                    sch.op("tensor", lambda rc=rc, fbs=fbs: [
                               nc.tensor.matmul(PS[:, 2048 + (fb % 16) * 128:2176 + (fb % 16) * 128],
                                                v1[:, rc, fb * 128:(fb + 1) * 128], pbf[:, rc, :],
                                                start=(rc == 0 and fb % 4 == 0),
                                                stop=(rc == 1 and fb % 4 == 3)) for fb in fbs][-1],
                           reads=["pbf", "v1"], pw=[(2048, 4096)], tag=f"h_{I}_{half}_{rc}")
                sch.op("scalar", lambda fbs=fbs: [
                           nc.scalar.activation(out=Hbf[:, fb, :],
                                                in_=PS[:, 2048 + (fb % 16) * 128:2176 + (fb % 16) * 128],
                                                func=AF.Gelu_apprx_tanh, bias=b1t[:, fb:fb + 1]) for fb in fbs][-1],
                       reads=["b1t"], writes=[f"Hbf{half}"], pr=[(2048, 4096)], tag=f"gelu_{I}_{half}")
                for fb in fbs:
                    c0 = qf_cols[fb % 4]
                    sch.op("tensor", lambda fb=fb, c0=c0, half=half: (
                               nc.tensor.matmul(PS[:, c0:c0 + 128], u2[:, fb, 0:128], Hbf[:, fb, :],
                                                start=(fb < 4), stop=False),
                               nc.tensor.matmul(PS[:, c0 + 128:c0 + 256], u2[:, fb, 128:256], Hbf[:, fb, :],
                                                start=False, stop=(fb >= 28)))[-1],
                           reads=[f"Hbf{half}", "u2"], pw=[(c0, c0 + 256)], tag=f"u2_{I}_{fb}")

            sch.op("vector", lambda: nc.vector.tensor_copy(out=qftmp[:], in_=PS[:, 0:256]),
                   writes=["qftmp"], pr=[(0, 256)], tag=f"qfc1_{I}")
            sch.op("vector", lambda: nc.vector.tensor_tensor(out=qftmp[:], in0=qftmp[:], in1=PS[:, 512:768], op=ALU.add),
                   reads=["qftmp"], writes=["qftmp"], pr=[(512, 768)], tag=f"qfc2_{I}")
            sch.op("vector", lambda: nc.vector.tensor_tensor(out=qftmp[:], in0=qftmp[:], in1=PS[:, 1024:1280], op=ALU.add),
                   reads=["qftmp"], writes=["qftmp"], pr=[(1024, 1280)], tag=f"qfc3_{I}")
            sch.op("vector", lambda: nc.vector.tensor_tensor(out=qfbf[:], in0=qftmp[:], in1=PS[:, 1536:1792], op=ALU.add),
                   reads=["qftmp"], writes=["qfbf"], pr=[(1536, 1792)], tag=f"qfc4_{I}")
            tap("qfbf", I, lambda: qfbf[:], "qfbf")

            sch.op("tensor", lambda: (nc.tensor.matmul(PS[:, 3072:3584], qfbf[:, 0, :], v2[:, 0, 0:512],
                                                       start=True, stop=False),
                                      nc.tensor.matmul(PS[:, 3584:4096], qfbf[:, 0, :], v2[:, 0, 512:1024],
                                                       start=True, stop=False))[-1],
                   reads=["qfbf", "v2"], pw=[(3072, 4096)], tag=f"v2a_{I}")
            sch.op("tensor", lambda: (nc.tensor.matmul(PS[:, 3072:3584], qfbf[:, 1, :], v2[:, 1, 0:512],
                                                       start=False, stop=False),
                                      nc.tensor.matmul(PS[:, 3584:4096], qfbf[:, 1, :], v2[:, 1, 512:1024],
                                                       start=False, stop=False))[-1],
                   reads=["qfbf", "v2"], pw=[(3072, 4096)], tag=f"v2b_{I}")
            sch.op("tensor", lambda: (nc.tensor.matmul(PS[:, 3072:3584], onesb[:], b2r[0:1, 0:512],
                                                       start=False, stop=True),
                                      nc.tensor.matmul(PS[:, 3584:4096], onesb[:], b2r[0:1, 512:1024],
                                                       start=False, stop=True))[-1],
                   reads=["ones", "b2r"], pw=[(3072, 4096)], tag=f"b2_{I}")

            emit_ln_stats(I, p, lambda: PS[:, 3072:4096], [], [(3072, 4096)], 3, "4")
            sch.op("vector", lambda p=p: nc.vector.tensor_scalar(out=scr[:, p, 2048:3072], in0=PS[:, 3072:4096],
                                                                 scalar1=rstd[:, 3:4], scalar2=negmr[:, 3:4],
                                                                 op0=ALU.mult, op1=ALU.add),
                   reads=["rstd3", "negmr3"], writes=[f"sC{p}"], pr=[(3072, 4096)], tag=f"ln4ap_{I}")
            sch.op("gpsimd", lambda p=p: nc.gpsimd.tensor_tensor(out=outb[p][:], in0=scr[:, p, 2048:3072],
                                                                 in1=x1[p][:], op=ALU.add),
                   reads=[f"sC{p}", f"x1{p}"], writes=[f"out{p}"], tag=f"resid2_{I}")
            sch.op("scalar", lambda I=I, p=p: nc.scalar.dma_start(out=y_h[I * C:(I + 1) * C, :], in_=outb[p][:]),
                   reads=[f"out{p}"], dma=True, tag=f"store_{I}")

        block = E(nc.Block())
        sch.run(block)

    return nc, dbg_names


def kernel(L0_attn_X, L0_attn_W_Q, L0_attn_W_K, L0_attn_W_V, L0_attn_W_G, L0_attn_W_O,
           L0_attn_gn_weight, L0_attn_gn_bias, L0_ffn_U1, L0_ffn_V1, L0_ffn_b1,
           L0_ffn_U2, L0_ffn_V2, L0_ffn_b2):
    consts, gc_list = host_constants()
    nc, _ = build_nc(gc_list)
    common = dict(
        wq=np.ascontiguousarray(L0_attn_W_Q, np.float32),
        wk=np.ascontiguousarray(L0_attn_W_K, np.float32),
        wv=np.ascontiguousarray(L0_attn_W_V, np.float32),
        wg=np.ascontiguousarray(L0_attn_W_G, np.float32),
        wo=np.ascontiguousarray(L0_attn_W_O, np.float32),
        gnw=np.ascontiguousarray(L0_attn_gn_weight, np.float32),
        gnb=np.ascontiguousarray(L0_attn_gn_bias, np.float32),
        u1=np.ascontiguousarray(L0_ffn_U1, np.float32),
        v1=np.ascontiguousarray(L0_ffn_V1, np.float32),
        b1=np.ascontiguousarray(L0_ffn_b1, np.float32),
        u2=np.ascontiguousarray(L0_ffn_U2, np.float32),
        v2=np.ascontiguousarray(L0_ffn_V2, np.float32),
        b2=np.ascontiguousarray(L0_ffn_b2, np.float32),
        **consts,
    )
    X = np.ascontiguousarray(L0_attn_X, np.float32)
    in_maps = [dict(common, x=np.ascontiguousarray(X[c])) for c in range(8)]
    res = run_bass_kernel_spmd(nc, in_maps, core_ids=list(range(8)))
    return np.stack([np.asarray(r["y"], np.float32) for r in res.results], axis=0)
